# version 03
# baseline (speedup 1.0000x reference)
import numpy as np
from contextlib import ExitStack

import concourse.bass as bass
import concourse.mybir as mybir
from concourse.bass_utils import run_bass_kernel_spmd

F32 = mybir.dt.float32
F16 = mybir.dt.float16
AF = mybir.ActivationFunctionType
OP = mybir.AluOpType

R = 8
T = 8192
D = 4096
M = 14336
NS = 16
RK = 128
TSH = T // R
DSH = D // R
NB = DSH // 128
QN = 2
TQ = TSH // QN
MB = M // 128
KB = D // 128
KH = KB // 2
CN = 8
TM = T // CN
NBAT = 4
NJ = MB // NBAT
DEBUG = False
SKIP = set()

GI_US = 13.7
MCH_US = 545.0
DN0 = 2 * MB * GI_US + 60.0


class Ch:
    def __init__(self, stack, nc, name):
        self.sem = stack.enter_context(nc.semaphore(name))
        self.n = 0

    def bump(self, k=1):
        self.n += k
        return self.n


class Plan:
    def __init__(self):
        self.items = []
        self.seq = 0

    def em(self, t, eng, fn):
        self.items.append((t, self.seq, eng, fn))
        self.seq += 1

    def run(self, eng_name, eng):
        for t, s, e, fn in sorted(self.items):
            if e == eng_name:
                fn(eng)


def build_nc():
    nc = bass.Bass("TRN2", target_bir_lowering=False)

    d_xt = nc.dram_tensor("x_t", [TSH, D], F32, kind="ExternalInput")
    d_xTd = nc.dram_tensor("xT_d", [DSH, T], F32, kind="ExternalInput")
    d_wg = nc.dram_tensor("wg_t", [MB, 2, 128, KH * 128], F16, kind="ExternalInput")
    d_wu = nc.dram_tensor("wu_t", [MB, 2, 128, KH * 128], F16, kind="ExternalInput")
    d_wd = nc.dram_tensor("wd_t", [NJ, 128, NBAT * NB * 128], F16, kind="ExternalInput")
    d_xpw = nc.dram_tensor("xpwT", [D, 160], F32, kind="ExternalInput")
    d_dtw = nc.dram_tensor("dtwT", [RK, DSH], F32, kind="ExternalInput")
    d_dtb = nc.dram_tensor("dtb", [128, NB], F32, kind="ExternalInput")
    d_Dd = nc.dram_tensor("Dd", [128, NB], F32, kind="ExternalInput")
    d_Al = nc.dram_tensor("Alog", [128, NB * NS], F32, kind="ExternalInput")
    d_out = nc.dram_tensor("out_dT", [DSH, T], F32, kind="ExternalOutput")

    xn16d = nc.dram_tensor("xn16d", [TSH, D], F16)
    s_dram = nc.dram_tensor("s_dram", [DSH, T], F32)
    cst_i = nc.dram_tensor("cst_i", [1, 2 * TSH], F32)
    cst_o = nc.dram_tensor("cst_o", [R, 2 * TSH], F32)
    cxp_i = nc.dram_tensor("cxp_i", [160, TSH], F32)
    cxp_o = nc.dram_tensor("cxp_o", [R * 160, TSH], F32)
    cht_i = [nc.dram_tensor(f"cht_i{q}", [M, TQ], F16) for q in range(QN)]
    cht_o = [nc.dram_tensor(f"cht_o{q}", [R * M, TQ], F16) for q in range(QN)]
    if DEBUG:
        dbg_xn = nc.dram_tensor("dbg_xn", [TSH, D], F16, kind="ExternalOutput")
        dbg_s = nc.dram_tensor("dbg_s", [DSH, T], F32, kind="ExternalOutput")
        dbg_ht = nc.dram_tensor("dbg_ht", [M, TQ], F16, kind="ExternalOutput")
        dbg_xp = nc.dram_tensor("dbg_xp", [R * 160, TSH], F32,
                                kind="ExternalOutput")

    rg = [list(range(R))]

    with ExitStack() as outer:
        sb = lambda name, shape, dt: outer.enter_context(
            nc.sbuf_tensor(name, shape, dt))
        xnT16 = sb("xnT16", [128, KB * TQ], F16)
        A_sb = sb("A_sb", [128, NB * NS], F32)
        D_sb = sb("D_sb", [128, NB], F32)
        dtb_sb = sb("dtb_sb", [128, NB], F32)
        dtw_sb = sb("dtw_sb", [128, DSH], F32)
        carry = sb("carry", [128, NB * NS], F32)
        cst_sb = sb("cst_sb", [128, 2], F32)

        ch = lambda name: Ch(outer, nc, name)
        c_st = ch("c_st")
        c_xp = ch("c_xp")

        with ExitStack() as b1:
            sb1 = lambda name, shape, dt: b1.enter_context(
                nc.sbuf_tensor(name, shape, dt))
            xti = [sb1(f"xti{j}", [128, D], F32) for j in range(2)]
            xsq = sb1("xsq", [128, D], F32)
            xn16t = sb1("xn16t", [128, D], F16)
            colA = sb1("colA", [128, 8], F32)
            stats_sb = sb1("stats_sb", [128, 16], F32)
            xpw16 = sb1("xpw16", [128, KB * 160], F16)
            xpwst = sb1("xpwst", [128, 160], F32)
            stg1 = sb1("stg1", [128, TQ], F32)
            stg2 = sb1("stg2", [32, TQ], F32)
            p_x1 = b1.enter_context(nc.psum_tensor("p_x1", [128, TQ], F32))
            p_x2 = b1.enter_context(nc.psum_tensor("p_x2", [32, TQ], F32))

            l_xt = [ch("l_xt0"), ch("l_xt1")]; l_xn = ch("l_xn"); l_tr = ch("l_tr")
            l_st = ch("l_st"); l_xpw = ch("l_xpw"); l_xps = ch("l_xps")
            v1 = ch("v1"); s1 = ch("s1"); pe1 = ch("pe1"); g1 = ch("g1")
            vsm = ch("vsm"); ssm = ch("ssm")

            NT = TSH // 128
            with nc.Block() as blk:
                @blk.sync
                def _(s):
                    for i in range(NT):
                        if i >= 2:
                            s.wait_ge(v1.sem, 3 * (i - 1))
                        s.dma_start(out=xti[i % 2][:, :],
                                    in_=d_xt.ap()[i * 128:(i + 1) * 128, :]
                                    ).then_inc(l_xt[i % 2].sem, 16)
                        l_xt[i % 2].bump(16)
                        s.wait_ge(v1.sem, 3 * i + 3)
                        s.dma_start(out=xn16d.ap()[i * 128:(i + 1) * 128, :],
                                    in_=xn16t[:, :]).then_inc(l_xn.sem, 16)
                        l_xn.bump(16)
                    s.wait_ge(l_xn.sem, 16 * NT)
                    for q in range(QN):
                        if q >= 1:
                            s.wait_ge(pe1.sem, 2 * q)
                        for db in range(KB):
                            s.dma_start(
                                out=xnT16[:, db * TQ:(db + 1) * TQ],
                                in_=xn16d.ap()[q * TQ:(q + 1) * TQ,
                                               db * 128:(db + 1) * 128],
                                transpose=True).then_inc(l_tr.sem, 16)
                            l_tr.bump(16)

                @blk.vector
                def _(v):
                    for i in range(NT):
                        xt = xti[i % 2]
                        v.wait_ge(l_xt[i % 2].sem, 16 * (i // 2 + 1))
                        def step(ins_):
                            ins_.then_inc(vsm.sem, 1)
                            v.wait_ge(vsm.sem, vsm.bump())
                        step(v.reduce_sum(colA[:, 0:1], xt[:, :],
                                          axis=mybir.AxisListType.X))
                        v.wait_ge(s1.sem, 2 * i + 1)
                        step(v.reduce_sum(colA[:, 1:2], xsq[:, :],
                                          axis=mybir.AxisListType.X))
                        step(v.tensor_scalar_mul(stats_sb[:, 2 * i:2 * i + 1],
                                                 colA[:, 0:1], -1.0 / D))
                        step(v.tensor_scalar(colA[:, 3:4],
                                             stats_sb[:, 2 * i:2 * i + 1],
                                             stats_sb[:, 2 * i:2 * i + 1],
                                             None, OP.mult))
                        step(v.tensor_scalar_mul(colA[:, 4:5], colA[:, 1:2],
                                                 1.0 / D))
                        step(v.scalar_tensor_tensor(colA[:, 5:6], colA[:, 3:4],
                                                    -1.0, colA[:, 4:5],
                                                    OP.mult, OP.add))
                        v.tensor_scalar_add(colA[:, 5:6], colA[:, 5:6], 1e-5
                                            ).then_inc(v1.sem, 1)
                        v1.bump()
                        v.wait_ge(s1.sem, 2 * i + 2)
                        v.engine_nop().then_inc(v1.sem, 1)
                        v1.bump()
                        if i >= 1:
                            v.wait_ge(l_xn.sem, 16 * i)
                        v.tensor_scalar(xn16t[:, :], xt[:, :],
                                        stats_sb[:, 2 * i:2 * i + 1],
                                        stats_sb[:, 2 * i + 1:2 * i + 2],
                                        OP.add, OP.mult).then_inc(v1.sem, 1)
                        v1.bump()
                    v.wait_ge(s1.sem, 2 * NT + KB + 1)
                    v.tensor_scalar_mul(A_sb[:, :], A_sb[:, :], -1.0
                                        ).then_inc(vsm.sem, 1)
                    v.wait_ge(vsm.sem, vsm.bump())
                    v.memset(carry[:, :], 0.0).then_inc(v1.sem, 1)
                    v1.bump()

                @blk.scalar
                def _(sc):
                    for i in range(NT):
                        sc.wait_ge(l_xt[i % 2].sem, 16 * (i // 2 + 1))
                        if i >= 1:
                            sc.wait_ge(v1.sem, 3 * (i - 1) + 1)
                        sc.activation(xsq[:, :], xti[i % 2][:, :], AF.Square
                                      ).then_inc(s1.sem, 1)
                        s1.bump()
                        sc.wait_ge(v1.sem, 3 * i + 1)
                        if i == 0:
                            sc.wait_ge(g1.sem, 1)
                        sc.activation(colA[:, 6:7], colA[:, 5:6], AF.Ln
                                      ).then_inc(ssm.sem, 1)
                        sc.wait_ge(ssm.sem, ssm.bump())
                        sc.activation(stats_sb[:, 2 * i + 1:2 * i + 2],
                                      colA[:, 6:7], AF.Exp,
                                      scale=cst_sb[:, 1:2]).then_inc(s1.sem, 1)
                        s1.bump()
                    sc.wait_ge(v1.sem, 3 * (NT - 1) + 1)
                    for i in range(NT):
                        sc.dma_start(out=cst_i.ap()[0:1, i * 128:(i + 1) * 128],
                                     in_=stats_sb[:, 2 * i:2 * i + 1]
                                     ).then_inc(l_st.sem, 16)
                        l_st.bump(16)
                        sc.dma_start(
                            out=cst_i.ap()[0:1, TSH + i * 128:TSH + (i + 1) * 128],
                            in_=stats_sb[:, 2 * i + 1:2 * i + 2]
                            ).then_inc(l_st.sem, 16)
                        l_st.bump(16)
                    for kb in range(KB):
                        sc.dma_start(out=xpwst[:, :],
                                     in_=d_xpw.ap()[kb * 128:(kb + 1) * 128, :]
                                     ).then_inc(l_xpw.sem, 16)
                        l_xpw.bump(16)
                        sc.wait_ge(l_xpw.sem, l_xpw.n)
                        sc.copy(xpw16[:, kb * 160:(kb + 1) * 160], xpwst[:, :]
                                ).then_inc(s1.sem, 1)
                        s1.bump()
                    sc.dma_start(out=A_sb[:, :], in_=d_Al.ap()[:, :]
                                 ).then_inc(l_xpw.sem, 16)
                    l_xpw.bump(16)
                    sc.dma_start(out=D_sb[:, :], in_=d_Dd.ap()[:, :]
                                 ).then_inc(l_xpw.sem, 16)
                    l_xpw.bump(16)
                    sc.dma_start(out=dtb_sb[:, :], in_=d_dtb.ap()[:, :]
                                 ).then_inc(l_xpw.sem, 16)
                    l_xpw.bump(16)
                    sc.dma_start(out=dtw_sb[:, :], in_=d_dtw.ap()[:, :]
                                 ).then_inc(l_xpw.sem, 16)
                    l_xpw.bump(16)
                    sc.wait_ge(l_xpw.sem, l_xpw.n)
                    sc.activation(A_sb[:, :], A_sb[:, :], AF.Exp
                                  ).then_inc(s1.sem, 1)
                    s1.bump()
                    for q in range(QN):
                        sc.wait_ge(pe1.sem, 2 * q + 2)
                        sc.copy(stg1[:, :], p_x1[:, :])
                        sc.copy(stg2[:, :], p_x2[:, :]).then_inc(s1.sem, 1)
                        s1.bump()
                        sc.dma_start(out=cxp_i.ap()[0:128, q * TQ:(q + 1) * TQ],
                                     in_=stg1[:, :]).then_inc(l_xps.sem, 16)
                        l_xps.bump(16)
                        sc.dma_start(out=cxp_i.ap()[128:160, q * TQ:(q + 1) * TQ],
                                     in_=stg2[:, :]).then_inc(l_xps.sem, 16)
                        l_xps.bump(16)

                n_s1c = 2 * NT + KB + 1

                @blk.tensor
                def _(t):
                    t.wait_ge(s1.sem, n_s1c)
                    for q in range(QN):
                        t.wait_ge(l_tr.sem, 16 * 32 * (q + 1))
                        t.wait_ge(s1.sem, n_s1c + q)
                        for kb in range(KB):
                            lt = xpw16[:, kb * 160:kb * 160 + 128]
                            lt2 = xpw16[:, kb * 160 + 128:(kb + 1) * 160]
                            rh = xnT16[:, kb * TQ:(kb + 1) * TQ]
                            t.matmul(p_x1[:, :], lt, rh, start=(kb == 0),
                                     stop=(kb == KB - 1))
                            mm = t.matmul(p_x2[:, :], lt2, rh, start=(kb == 0),
                                          stop=(kb == KB - 1))
                        mm.then_inc(pe1.sem, 2)
                        pe1.bump(2)

                @blk.gpsimd
                def _(g):
                    g.memset(cst_sb[:, 0:1], 0.5)
                    g.memset(cst_sb[:, 1:2], -0.5).then_inc(g1.sem, 1)
                    g1.bump()
                    g.wait_ge(l_st.sem, 16 * 16)
                    g.collective_compute("AllGather", OP.bypass, rg,
                                         [cst_i.ap().opt()], [cst_o.ap().opt()]
                                         ).then_inc(c_st.sem, 1)
                    c_st.bump()
                    g.wait_ge(l_xps.sem, 16 * 4)
                    g.collective_compute("AllGather", OP.bypass, rg,
                                         [cxp_i.ap().opt()], [cxp_o.ap().opt()]
                                         ).then_inc(c_xp.sem, 1)
                    c_xp.bump()

        with ExitStack() as b2:
            sb2 = lambda name, shape, dt: b2.enter_context(
                nc.sbuf_tensor(name, shape, dt))
            wg_sb = [sb2(f"wg{j}", [128, KH * 128], F16) for j in range(2)]
            wu_sb = [sb2(f"wu{j}", [128, KH * 128], F16) for j in range(2)]
            wd_sb = [sb2(f"wd{j}", [128, NBAT * NB * 128], F16) for j in range(2)]
            ht_sb = [sb2(f"ht{j}", [128, NBAT * TQ], F16) for j in range(2)]
            thg = [sb2(f"thg{j}", [128, TQ], F32) for j in range(2)]
            t1g = sb2("t1g", [128, TQ], F32)
            h16 = [sb2(f"h16{j}", [128, TQ], F16) for j in range(2)]
            thd = [sb2(f"thd{j}", [128, TQ], F32) for j in range(2)]
            t1d = sb2("t1d", [128, TQ], F32)
            s_tl = [sb2(f"s_tl{j}", [128, TQ], F32) for j in range(2)]
            xT_tl = [sb2(f"xT_tl{j}", [128, TQ], F32) for j in range(2)]
            o_sb = [sb2(f"o_sb{j}", [128, TQ], F32) for j in range(2)]
            xn_sb = sb2("xn_sb", [128, NB * TM], F32)
            dt_sb = [sb2(f"dt{j}", [128, NB * TM], F32) for j in range(2)]
            dtx_sb = sb2("dtx", [128, NB * TM], F32)
            sp_o = [sb2(f"sp_o{j}", [128, TQ], F32) for j in range(2)]
            sp_tmp = sb2("sp_tmp", [128, TQ], F32)
            dA_r = [sb2(f"dA{j}", [128, TM], F32) for j in range(2)]
            bh_r = [sb2(f"bh{j}", [128, TM], F32) for j in range(2)]
            B_r = [sb2(f"B{j}", [128, TM], F32) for j in range(2)]
            C_r = [sb2(f"C{j}", [128, TM], F32) for j in range(2)]
            mr_rep = sb2("mr_rep", [128, TM], F32)
            dtin = sb2("dtin", [128, TM], F32)
            hc_tmp = sb2("hc_tmp", [128, TM], F32)

            psum = [b2.enter_context(nc.psum_tensor(f"pb{i}", [128, TQ], F32))
                    for i in range(8)]
            pg, pu, pd = psum[0:2], psum[2:4], psum[4:8]

            d_wgc = [ch("d_wgc0"), ch("d_wgc1")]
            d_wuc = [ch("d_wuc0"), ch("d_wuc1")]
            d_wdc = [ch("d_wdc0"), ch("d_wdc1")]
            d_htr = [ch("d_htr0"), ch("d_htr1")]
            d_hts = [ch("d_hts0"), ch("d_hts1")]
            d_tr2 = ch("d_tr2")
            d_mld = ch("d_mld")
            d_repB = [ch("d_repB0"), ch("d_repB1")]
            d_repC = [ch("d_repC0"), ch("d_repC1")]
            d_dti = ch("d_dti")
            d_ssw = [ch(f"d_ssw{b}") for b in range(NB)]
            d_stl = [ch("d_stl0"), ch("d_stl1")]
            d_xtt = [ch("d_xtt0"), ch("d_xtt1")]
            d_outc = [ch("d_outc0"), ch("d_outc1")]
            c_h = [ch("c_h0"), ch("c_h1")]
            pe_g = ch("pe_g"); pe_u = ch("pe_u"); pe_dn = ch("pe_dn")
            pe_dnb = ch("pe_dnb"); pe_dt = ch("pe_dt")
            ev_g = ch("ev_g"); ev_u = ch("ev_u"); ev_dn = ch("ev_dn")
            ev_dtb = ch("ev_dtb")
            sc_th = ch("sc_th"); sc_thd = ch("sc_thd"); sc_sp = ch("sc_sp")
            sc_sp2 = ch("sc_sp2")
            sc_dA = ch("sc_dA"); sc_car = ch("sc_car")
            v_scan = ch("v_scan"); v_yin = ch("v_yin"); v_sp = ch("v_sp")
            v_dnev = ch("v_dnev"); v_prep = ch("v_prep")
            gp_hc = ch("gp_hc"); gp_acc = ch("gp_acc")

            P = Plan()

            for q in range(QN if 'gateup' not in SKIP else 0):
                def refill(s, q=q):
                    if q >= 1:
                        s.wait_ge(pe_g.sem, 2 * MB)
                        s.wait_ge(pe_u.sem, 2 * MB)
                    for db in range(KB):
                        s.dma_start(
                            out=xnT16[:, db * TQ:(db + 1) * TQ],
                            in_=xn16d.ap()[q * TQ:(q + 1) * TQ,
                                           db * 128:(db + 1) * 128],
                            transpose=True).then_inc(d_tr2.sem, 16)
                P.em(q * MB * GI_US - 32.0, 'sync', refill)
                d_tr2.bump(16 * KB)

                for mb in range(MB):
                    gi = q * MB + mb
                    tt = gi * GI_US
                    for h in range(2):
                        j = 2 * gi + h
                        def wload(s, q=q, mb=mb, h=h, j=j):
                            if j >= 2:
                                s.wait_ge(pe_g.sem, j - 1)
                            s.dma_start(
                                out=wg_sb[j % 2][:, :],
                                in_=d_wg.ap()[mb, h, :, :]
                                ).then_inc(d_wgc[j % 2].sem, 16)
                            if j >= 2:
                                s.wait_ge(pe_u.sem, j - 1)
                            s.dma_start(
                                out=wu_sb[j % 2][:, :],
                                in_=d_wu.ap()[mb, h, :, :]
                                ).then_inc(d_wuc[j % 2].sem, 16)
                        P.em(tt - 30.0 + h * 3.0, 'sync', wload)
                        d_wgc[j % 2].bump(16); d_wuc[j % 2].bump(16)

                    def gup(t, q=q, mb=mb, gi=gi):
                        t.wait_ge(d_tr2.sem, 16 * KB * (q + 1))
                        for h in range(2):
                            j = 2 * gi + h
                            t.wait_ge(d_wgc[j % 2].sem, 16 * (j // 2 + 1))
                            if gi >= 2 and h == 0:
                                t.wait_ge(ev_g.sem, gi - 1)
                            for k in range(KH):
                                kb = h * KH + k
                                mm = t.matmul(
                                    pg[gi % 2][:, :],
                                    wg_sb[j % 2][:, k * 128:(k + 1) * 128],
                                    xnT16[:, kb * TQ:(kb + 1) * TQ],
                                    start=(kb == 0), stop=(kb == KB - 1))
                            mm.then_inc(pe_g.sem, 1)
                        for h in range(2):
                            j = 2 * gi + h
                            t.wait_ge(d_wuc[j % 2].sem, 16 * (j // 2 + 1))
                            if gi >= 2 and h == 0:
                                t.wait_ge(ev_u.sem, gi - 1)
                            for k in range(KH):
                                kb = h * KH + k
                                mm = t.matmul(
                                    pu[gi % 2][:, :],
                                    wu_sb[j % 2][:, k * 128:(k + 1) * 128],
                                    xnT16[:, kb * TQ:(kb + 1) * TQ],
                                    start=(kb == 0), stop=(kb == KB - 1))
                            mm.then_inc(pe_u.sem, 1)
                    P.em(tt, 'tensor', gup)
                    pe_g.bump(2); pe_u.bump(2)

                    def tanh_g(sc, gi=gi):
                        sc.wait_ge(pe_g.sem, 2 * gi + 2)
                        if gi >= 2:
                            sc.wait_ge(ev_g.sem, gi - 1)
                        sc.activation(thg[gi % 2][:, :], pg[gi % 2][:, :],
                                      AF.Tanh, scale=cst_sb[:, 0:1]
                                      ).then_inc(sc_th.sem, 1)
                    P.em(tt + 6.0, 'scalar', tanh_g)
                    sc_th.bump()

                    def t1_g(v, gi=gi):
                        v.wait_ge(sc_th.sem, gi + 1)
                        v.scalar_tensor_tensor(t1g[:, :], thg[gi % 2][:, :],
                                               1.0, pg[gi % 2][:, :], OP.add,
                                               OP.mult).then_inc(ev_g.sem, 1)
                        v.wait_ge(pe_u.sem, 2 * gi + 2)
                        if gi >= 2:
                            v.wait_ge(d_hts[gi % 2].sem, 16 * (gi // 2))
                        v.scalar_tensor_tensor(h16[gi % 2][:, :], t1g[:, :],
                                               0.5, pu[gi % 2][:, :], OP.mult,
                                               OP.mult).then_inc(ev_u.sem, 1)
                    P.em(tt + 8.0, 'vector', t1_g)
                    ev_g.bump(); ev_u.bump()

                    def hts(sc, q=q, mb=mb, gi=gi):
                        sc.wait_ge(ev_u.sem, gi + 1)
                        sc.dma_start(
                            out=cht_i[q].ap()[mb * 128:(mb + 1) * 128, :],
                            in_=h16[gi % 2][:, :]
                            ).then_inc(d_hts[gi % 2].sem, 16)
                    P.em(tt + 10.0, 'scalar', hts)
                    d_hts[gi % 2].bump(16)

            for q in range(QN if 'gateup' not in SKIP else 0):
                def agh(g, q=q):
                    g.wait_ge(d_hts[0].sem, 16 * (MB // 2) * (q + 1))
                    g.wait_ge(d_hts[1].sem, 16 * (MB // 2) * (q + 1))
                    g.collective_compute(
                        "AllGather", OP.bypass, rg,
                        [cht_i[q].ap().opt()], [cht_o[q].ap().opt()]
                        ).then_inc(c_h[q].sem, 1)
                P.em((4.0 + 1.6 * q) * MCH_US, 'gpsimd', agh)
                c_h[q].bump()

            scan_i = 0
            for c in range(CN if 'mamba' not in SKIP else 0):
                tc = c * MCH_US
                def dtin_ld(sc, c=c):
                    sc.wait_ge(c_xp.sem, 1)
                    if c >= 1:
                        sc.wait_ge(pe_dt.sem, 2 * NB * c)
                    sc.dma_start(out=dtin[:, :],
                                 in_=cxp_o.ap()[160 * c:160 * c + 128, :]
                                 ).then_inc(d_dti.sem, 16)
                P.em(tc - MCH_US + 10.0, 'scalar', dtin_ld)
                d_dti.bump(16)

                for b in range(NB):
                    def dt_mm(t, c=c, b=b):
                        t.wait_ge(d_dti.sem, 16 * (c + 1))
                        bank = pd if c < CN - 1 else pg
                        if c == CN - 1 and b == 0 and 'gateup' not in SKIP:
                            t.wait_ge(ev_g.sem, QN * MB)
                        for h in range(2):
                            j = (c * NB + b) * 2 + h
                            t.wait_ge(ev_dtb.sem, max(0, j - 1))
                            t.matmul(bank[j % 2][:, :],
                                     dtw_sb[:, b * 128:(b + 1) * 128],
                                     dtin[:, h * TQ:(h + 1) * TQ],
                                     start=True, stop=True
                                     ).then_inc(pe_dt.sem, 1)
                    P.em(tc - MCH_US + 17.0 + b, 'tensor', dt_mm)
                    pe_dt.bump(2)

                for b in range(NB):
                    def sp_sc(sc, c=c, b=b):
                        ib = c * NB + b
                        sc.wait_ge(pe_dt.sem, 2 * ib + 2)
                        if ib >= 1:
                            sc.wait_ge(v_sp.sem, 2 * ib)
                        if c >= 2:
                            sc.wait_ge(sc_dA.sem, (c - 1) * NS * NB)
                        bank = pd if c < CN - 1 else pg
                        for h in range(2):
                            j = ib * 2 + h
                            cs = b * TM + h * TQ
                            sc.activation(dt_sb[c % 2][:, cs:cs + TQ],
                                          bank[j % 2][:, :], AF.Relu,
                                          bias=dtb_sb[:, b:b + 1])
                            sc.activation(sp_o[h][:, :], bank[j % 2][:, :],
                                          AF.Abs, bias=dtb_sb[:, b:b + 1]
                                          ).then_inc(ev_dtb.sem, 1)
                            sc.wait_ge(ev_dtb.sem, ib * 2 + h + 1)
                            sc.activation(sp_tmp[:, :], sp_o[h][:, :], AF.Exp,
                                          scale=-1.0).then_inc(sc_sp2.sem, 1)
                            sc.wait_ge(sc_sp2.sem, sc_sp2.bump())
                            sc.activation(sp_o[h][:, :], sp_tmp[:, :], AF.Ln,
                                          bias=1.0).then_inc(sc_sp.sem, 1)
                    P.em(tc - MCH_US + 20.0 + b, 'scalar', sp_sc)
                    ev_dtb.bump(2)
                    for h in range(2):
                        cnt = sc_sp.bump()
                        def sp_v(v, c=c, b=b, h=h, cnt=cnt):
                            v.wait_ge(sc_sp.sem, cnt)
                            cs2 = b * TM + h * TQ
                            v.tensor_add(dt_sb[c % 2][:, cs2:cs2 + TQ],
                                         dt_sb[c % 2][:, cs2:cs2 + TQ],
                                         sp_o[h][:, :]).then_inc(v_sp.sem, 1)
                        P.em(tc - MCH_US + 20.6 + b + 0.1 * h, 'vector',
                             sp_v)
                    v_sp.bump(2)

                for b in range(NB):
                    def mldb(sc, c=c, b=b):
                        if c == 0:
                            sc.wait_ge(c_st.sem, 1)
                        else:
                            sc.wait_ge(d_ssw[b].sem, 16 * c)
                        sc.dma_start(out=xn_sb[:, b * TM:(b + 1) * TM],
                                     in_=d_xTd.ap()[b * 128:(b + 1) * 128,
                                                    c * TM:(c + 1) * TM]
                                     ).then_inc(d_mld.sem, 16)
                    P.em((tc - 540.0 + b) if c == 0 else (tc - 9.5 + b),
                         'scalar', mldb)
                    d_mld.bump(16)

                def mld_m(sc, c=c):
                    if c >= 1:
                        sc.wait_ge(v_prep.sem, 3 * c - 1)
                    sc.dma_start(out=mr_rep[:, :],
                                 in_=cst_o.ap()[c:c + 1, 0:TM
                                                ].broadcast_to((128, TM))
                                 ).then_inc(d_mld.sem, 16)
                P.em(tc - 5.0 if c > 0 else tc - 535.0, 'scalar', mld_m)
                d_mld.bump(16)

                def rld(sc, c=c):
                    sc.wait_ge(v_prep.sem, 3 * c + 1)
                    sc.dma_start(out=mr_rep[:, :],
                                 in_=cst_o.ap()[c:c + 1, TSH:TSH + TM
                                                ].broadcast_to((128, TM))
                                 ).then_inc(d_mld.sem, 16)
                P.em(tc + 0.2, 'scalar', rld)
                d_mld.bump(16)

                def prep1(v, c=c):
                    v.wait_ge(d_mld.sem, 16 * (c * (NB + 2) + NB + 1))
                    if c == 0:
                        v.wait_ge(v1.sem, v1.n)
                    for b in range(NB):
                        bs = b * TM
                        v.tensor_add(xn_sb[:, bs:bs + TM], xn_sb[:, bs:bs + TM],
                                     mr_rep[:, :])
                    v.engine_nop().then_inc(v_prep.sem, 1)
                P.em(tc + 0.1, 'vector', prep1)
                v_prep.bump()

                def prep2(v, c=c):
                    v.wait_ge(d_mld.sem, 16 * (c + 1) * (NB + 2))
                    for b in range(NB):
                        bs = b * TM
                        v.tensor_mul(xn_sb[:, bs:bs + TM], xn_sb[:, bs:bs + TM],
                                     mr_rep[:, :])
                    v.engine_nop().then_inc(v_prep.sem, 1)
                    for b in range(NB):
                        bs = b * TM
                        v.wait_ge(v_sp.sem, (c * NB + b + 1) * 2)
                        v.tensor_mul(dtx_sb[:, bs:bs + TM],
                                     dt_sb[c % 2][:, bs:bs + TM],
                                     xn_sb[:, bs:bs + TM])
                        v.tensor_scalar(xn_sb[:, bs:bs + TM],
                                        xn_sb[:, bs:bs + TM], D_sb[:, b:b + 1],
                                        None, OP.mult).then_inc(v_yin.sem, 1)
                    v.engine_nop().then_inc(v_prep.sem, 1)
                P.em(tc + 0.4, 'vector', prep2)
                v_prep.bump(2); v_yin.bump(NB)

                for n in range(NS):
                    ri = c * NS + n
                    def repl(sc, c=c, n=n, ri=ri):
                        if ri >= 2:
                            sc.wait_ge(v_scan.sem, (ri - 2) * NB + NB)
                            sc.wait_ge(gp_hc.sem, (ri - 2) * NB + NB)
                        sc.dma_start(out=B_r[n % 2][:, :],
                                     in_=cxp_o.ap()[160 * c + 128 + n:
                                                    160 * c + 129 + n, :
                                                    ].broadcast_to((128, TM))
                                     ).then_inc(d_repB[n % 2].sem, 16)
                        sc.dma_start(out=C_r[n % 2][:, :],
                                     in_=cxp_o.ap()[160 * c + 144 + n:
                                                    160 * c + 145 + n, :
                                                    ].broadcast_to((128, TM))
                                     ).then_inc(d_repC[n % 2].sem, 16)
                    P.em(tc + 0.2 + n * 30.0, 'scalar', repl)
                    d_repB[n % 2].bump(16); d_repC[n % 2].bump(16)

                    for b in range(NB):
                        i = scan_i
                        col = b * NS + n
                        def dA_f(sc, c=c, b=b, i=i, col=col):
                            if i >= 2:
                                sc.wait_ge(v_scan.sem, i - 1)
                            sc.wait_ge(v_sp.sem, (c * NB + b + 1) * 2)
                            sc.activation(dA_r[i % 2][:, :],
                                          dt_sb[c % 2][:, b * TM:(b + 1) * TM],
                                          AF.Exp, scale=A_sb[:, col:col + 1]
                                          ).then_inc(sc_dA.sem, 1)
                        P.em(tc + 1.0 + (n * NB + b) * 8.0, 'scalar', dA_f)
                        sc_dA.bump()

                        def dbx_scan(v, c=c, b=b, n=n, i=i, col=col, ri=ri):
                            v.wait_ge(d_repB[n % 2].sem, 16 * (ri // 2 + 1))
                            if i >= 2:
                                v.wait_ge(gp_hc.sem, i - 1)
                            v.tensor_mul(bh_r[i % 2][:, :],
                                         dtx_sb[:, b * TM:(b + 1) * TM],
                                         B_r[n % 2][:, :])
                            v.wait_ge(sc_dA.sem, i + 1)
                            if i >= NS * NB:
                                v.wait_ge(sc_car.sem, i - NS * NB + 1)
                            v.tensor_tensor_scan(bh_r[i % 2][:, :],
                                                 dA_r[i % 2][:, :],
                                                 bh_r[i % 2][:, :],
                                                 carry[:, col:col + 1],
                                                 OP.mult, OP.add
                                                 ).then_inc(v_scan.sem, 1)
                        P.em(tc + 2.0 + (n * NB + b) * 8.0, 'vector', dbx_scan)
                        v_scan.bump()

                        def car_f(sc, i=i, col=col):
                            sc.wait_ge(v_scan.sem, i + 1)
                            sc.copy(carry[:, col:col + 1],
                                    bh_r[i % 2][:, TM - 1:TM]
                                    ).then_inc(sc_car.sem, 1)
                        P.em(tc + 4.0 + (n * NB + b) * 8.0, 'scalar', car_f)
                        sc_car.bump()

                        def hc_f(g, c=c, b=b, n=n, i=i, ri=ri):
                            g.wait_ge(v_scan.sem, i + 1)
                            g.wait_ge(d_repC[n % 2].sem, 16 * (ri // 2 + 1))
                            g.tensor_mul(hc_tmp[:, :], bh_r[i % 2][:, :],
                                         C_r[n % 2][:, :]).then_inc(gp_hc.sem, 1)
                            if n == 0:
                                g.wait_ge(v_yin.sem, c * NB + b + 1)
                            g.tensor_add(xn_sb[:, b * TM:(b + 1) * TM],
                                         xn_sb[:, b * TM:(b + 1) * TM],
                                         hc_tmp[:, :]).then_inc(gp_acc.sem, 1)
                        P.em(tc + 5.0 + (n * NB + b) * 8.0, 'gpsimd', hc_f)
                        gp_hc.bump(); gp_acc.bump()
                        scan_i += 1

                for b in range(NB):
                    def sstore(sc, c=c, b=b):
                        sc.wait_ge(gp_acc.sem,
                                   c * NS * NB + (NS - 1) * NB + b + 1)
                        sc.dma_start(out=s_dram.ap()[b * 128:(b + 1) * 128,
                                                     c * TM:(c + 1) * TM],
                                     in_=xn_sb[:, b * TM:(b + 1) * TM]
                                     ).then_inc(d_ssw[b].sem, 16)
                    P.em(tc + MCH_US - 10.0 + b, 'scalar', sstore)
                    d_ssw[b].bump(16)

            nr_dn = 0 if 'down' in SKIP else (1 if 'down1' in SKIP else R)
            nq_dn = 1 if 'down1' in SKIP else QN
            for r in range(nr_dn):
                for q in range(nq_dn):
                    ti = 2 * r + q
                    tt = DN0 + ti * 96.0
                    for J in range(NJ):
                        bi = ti * NJ + J
                        def bload(s, q=q, r=r, J=J, bi=bi):
                            s.wait_ge(c_h[q].sem, 1)
                            if bi >= 2:
                                s.wait_ge(pe_dnb.sem, bi - 1)
                            if 'localht' in SKIP:
                                base2 = J * NBAT * 128
                                s.dma_start(
                                    out=ht_sb[bi % 2][:, :],
                                    in_=cht_i[q].ap()[base2:base2 + NBAT * 128,
                                                      :].rearrange(
                                        "(a p) t -> p a t", p=128)
                                    ).then_inc(d_htr[bi % 2].sem, 16)
                            else:
                                base = r * M + J * NBAT * 128
                                s.dma_start(
                                    out=ht_sb[bi % 2][:, :],
                                    in_=cht_o[q].ap()[base:base + NBAT * 128, :
                                                      ].rearrange(
                                        "(a p) t -> p a t", p=128)
                                    ).then_inc(d_htr[bi % 2].sem, 16)
                            s.dma_start(
                                out=wd_sb[bi % 2][:, :],
                                in_=d_wd.ap()[J, :, :]
                                ).then_inc(d_wdc[bi % 2].sem, 16)
                        P.em(tt + J * 3.4 - 20.0, 'sync', bload)
                        d_htr[bi % 2].bump(16); d_wdc[bi % 2].bump(16)

                        def dmm(t, q=q, r=r, J=J, bi=bi, ti=ti):
                            t.wait_ge(d_htr[bi % 2].sem, 16 * (bi // 2 + 1))
                            t.wait_ge(d_wdc[bi % 2].sem, 16 * (bi // 2 + 1))
                            if J == 0:
                                if ti == 0:
                                    t.wait_ge(pe_dt.sem, 2 * NB * (CN - 1))
                                    t.wait_ge(ev_dtb.sem, 2 * NB * (CN - 1))
                                else:
                                    t.wait_ge(ev_dn.sem, ti * NB)
                            for jj in range(NBAT):
                                for hb in range(NB):
                                    mm = t.matmul(
                                        pd[hb][:, :],
                                        wd_sb[bi % 2][:,
                                                      (jj * NB + hb) * 128:
                                                      (jj * NB + hb + 1) * 128],
                                        ht_sb[bi % 2][:, jj * TQ:(jj + 1) * TQ],
                                        start=(J == 0 and jj == 0),
                                        stop=(J == NJ - 1 and jj == NBAT - 1))
                            mm.then_inc(pe_dnb.sem, 1)
                        P.em(tt + J * 3.4, 'tensor', dmm)
                        pe_dnb.bump()

                    if 'noevac' in SKIP:
                        def evmin(v, ti=ti):
                            v.wait_ge(pe_dnb.sem, (ti + 1) * NJ)
                            for hb in range(NB):
                                v.tensor_copy(o_sb[0][:, :], pd[hb][:, :]
                                              ).then_inc(ev_dn.sem, 1)
                        P.em(DN0 + ti * 96.0 + 92.0, 'vector', evmin)
                        ev_dn.bump(NB)
                        continue
                    for hb in range(NB):
                        ei = ti * NB + hb
                        st_time = tt + 88.6 + hb * 2.0
                        n_prior = sum(1 for c2 in range(CN)
                                      if (c2 + 1) * MCH_US - 10.0 + hb < st_time)
                        n_prior = max(n_prior, r + 1)

                        def stile(sc, q=q, r=r, hb=hb, ei=ei, n_prior=n_prior):
                            sc.wait_ge(d_ssw[hb].sem, 16 * n_prior)
                            if ei >= 2:
                                sc.wait_ge(v_dnev.sem, ei - 1)
                            sc.dma_start(
                                out=s_tl[ei % 2][:, :],
                                in_=s_dram.ap()[hb * 128:(hb + 1) * 128,
                                                r * TSH + q * TQ:
                                                r * TSH + (q + 1) * TQ]
                                ).then_inc(d_stl[ei % 2].sem, 16)
                            sc.dma_start(
                                out=xT_tl[ei % 2][:, :],
                                in_=d_xTd.ap()[hb * 128:(hb + 1) * 128,
                                               r * TSH + q * TQ:
                                               r * TSH + (q + 1) * TQ]
                                ).then_inc(d_xtt[ei % 2].sem, 16)
                        P.em(st_time, 'scalar', stile)
                        d_stl[ei % 2].bump(16); d_xtt[ei % 2].bump(16)

                        def tanh_d(sc, ti=ti, hb=hb, ei=ei):
                            sc.wait_ge(pe_dnb.sem, (ti + 1) * NJ)
                            if ei >= 2:
                                sc.wait_ge(ev_dn.sem, ei - 1)
                            sc.activation(thd[ei % 2][:, :], pd[hb][:, :],
                                          AF.Tanh, scale=cst_sb[:, 0:1]
                                          ).then_inc(sc_thd.sem, 1)
                        P.em(tt + 90.0 + hb * 2.0, 'scalar', tanh_d)
                        sc_thd.bump()

                        def ev_d(v, ti=ti, hb=hb, ei=ei):
                            v.wait_ge(sc_thd.sem, ei + 1)
                            v.scalar_tensor_tensor(t1d[:, :], thd[ei % 2][:, :],
                                                   1.0, pd[hb][:, :], OP.add,
                                                   OP.mult).then_inc(ev_dn.sem, 1)
                            v.wait_ge(d_stl[ei % 2].sem, 16 * (ei // 2 + 1))
                            v.scalar_tensor_tensor(t1d[:, :], t1d[:, :], 0.5,
                                                   s_tl[ei % 2][:, :],
                                                   OP.mult, OP.mult)
                            v.wait_ge(d_xtt[ei % 2].sem, 16 * (ei // 2 + 1))
                            if ei >= 2:
                                v.wait_ge(d_outc[ei % 2].sem, 16 * (ei // 2))
                            v.tensor_add(o_sb[ei % 2][:, :], t1d[:, :],
                                         xT_tl[ei % 2][:, :]
                                         ).then_inc(v_dnev.sem, 1)
                        P.em(tt + 92.0 + hb * 2.0, 'vector', ev_d)
                        ev_dn.bump(); v_dnev.bump()

                        def ostore(sc, q=q, r=r, hb=hb, ei=ei):
                            sc.wait_ge(v_dnev.sem, ei + 1)
                            sc.dma_start(
                                out=d_out.ap()[hb * 128:(hb + 1) * 128,
                                               r * TSH + q * TQ:
                                               r * TSH + (q + 1) * TQ],
                                in_=o_sb[ei % 2][:, :]
                                ).then_inc(d_outc[ei % 2].sem, 16)
                        P.em(tt + 94.0 + hb * 2.0, 'scalar', ostore)
                        d_outc[ei % 2].bump(16)

            def fin_sc(sc):
                sc.wait_ge(d_outc[0].sem, d_outc[0].n)
                sc.wait_ge(d_outc[1].sem, d_outc[1].n)
            P.em(1e9, 'scalar', fin_sc)

            def fin_g(g):
                g.wait_ge(d_outc[0].sem, d_outc[0].n)
                g.wait_ge(d_outc[1].sem, d_outc[1].n)
                if DEBUG:
                    g.dma_start(out=dbg_xn.ap()[:, :], in_=xn16d.ap()[:, :]
                                ).then_inc(gp_hc.sem, 16)
                    g.dma_start(out=dbg_s.ap()[:, :], in_=s_dram.ap()[:, :]
                                ).then_inc(gp_hc.sem, 16)
                    g.dma_start(out=dbg_ht.ap()[:, :], in_=cht_i[0].ap()[:, :]
                                ).then_inc(gp_hc.sem, 16)
                    g.dma_start(out=dbg_xp.ap()[:, :], in_=cxp_o.ap()[:, :]
                                ).then_inc(gp_hc.sem, 16)
                    g.wait_ge(gp_hc.sem, gp_hc.n + 64)
            P.em(1e9, 'gpsimd', fin_g)

            with nc.Block() as blk2:
                @blk2.sync
                def _(s):
                    P.run('sync', s)

                @blk2.tensor
                def _(t):
                    P.run('tensor', t)

                @blk2.vector
                def _(v):
                    P.run('vector', v)

                @blk2.scalar
                def _(sc):
                    P.run('scalar', sc)

                @blk2.gpsimd
                def _(g):
                    P.run('gpsimd', g)

    return nc


def kernel(L0_attn_x, L0_attn_A_log, L0_attn_D, L0_attn_x_proj_w,
           L0_attn_dt_proj_w, L0_attn_dt_proj_b, L0_ffn_W_gate, L0_ffn_W_up,
           L0_ffn_W_down):
    x = np.ascontiguousarray(np.asarray(L0_attn_x, dtype=np.float32)
                             ).reshape(T, D)
    A_log = np.asarray(L0_attn_A_log, dtype=np.float32)
    Dv = np.asarray(L0_attn_D, dtype=np.float32)
    xpw = np.asarray(L0_attn_x_proj_w, dtype=np.float32)
    dtw = np.asarray(L0_attn_dt_proj_w, dtype=np.float32)
    dtb = np.asarray(L0_attn_dt_proj_b, dtype=np.float32)
    Wg = np.asarray(L0_ffn_W_gate)
    Wu = np.asarray(L0_ffn_W_up)
    Wd = np.asarray(L0_ffn_W_down)

    wg_t = np.ascontiguousarray(
        Wg.reshape(MB, 128, 2, KH, 128).transpose(0, 2, 4, 3, 1)
        ).reshape(MB, 2, 128, KH * 128)
    wu_t = np.ascontiguousarray(
        Wu.reshape(MB, 128, 2, KH, 128).transpose(0, 2, 4, 3, 1)
        ).reshape(MB, 2, 128, KH * 128)
    xpwT = np.ascontiguousarray(xpw.T)
    xT = x.T

    in_maps = []
    for c in range(R):
        ds = slice(c * DSH, (c + 1) * DSH)
        in_maps.append({
            "x_t": np.ascontiguousarray(x[c * TSH:(c + 1) * TSH]),
            "xT_d": np.ascontiguousarray(xT[ds]),
            "wg_t": wg_t,
            "wu_t": wu_t,
            "wd_t": np.ascontiguousarray(
                Wd[ds].reshape(NB, 128, NJ, NBAT, 128).transpose(
                    2, 4, 3, 0, 1)).reshape(NJ, 128, NBAT * NB * 128),
            "xpwT": xpwT,
            "dtwT": np.ascontiguousarray(dtw[ds].T),
            "dtb": np.ascontiguousarray(dtb[ds].reshape(NB, 128).T),
            "Dd": np.ascontiguousarray(Dv[ds].reshape(NB, 128).T),
            "Alog": np.ascontiguousarray(
                A_log[ds].reshape(NB, 128, NS).transpose(1, 0, 2
                                                         ).reshape(128, NB * NS)),
        })

    nc = build_nc()
    res = run_bass_kernel_spmd(nc, in_maps, core_ids=list(range(R)))
    outT = np.concatenate([r["out_dT"] for r in res.results], axis=0)
    out = np.ascontiguousarray(outT.T).reshape(1, T, D).astype(np.float32)
    if DEBUG:
        kernel._dbg = res.results
    return out


# version 04 (final)
# speedup vs baseline: 1.0598x; 1.0598x over previous
import numpy as np
from contextlib import ExitStack

import concourse.bass as bass
import concourse.mybir as mybir
from concourse.bass_utils import run_bass_kernel_spmd

F32 = mybir.dt.float32
F16 = mybir.dt.float16
AF = mybir.ActivationFunctionType
OP = mybir.AluOpType

R = 8
T = 8192
D = 4096
M = 14336
NS = 16
RK = 128
TSH = T // R
DSH = D // R
NB = DSH // 128
QN = 2
TQ = TSH // QN
MB = M // 128
KB = D // 128
KH = KB // 2
CN = 8
TM = T // CN
NBAT = 4
NJ = MB // NBAT
DEBUG = False
SKIP = set()

GI_US = 13.7
MCH_US = 545.0
DN0 = 2 * MB * GI_US + 60.0


class Ch:
    def __init__(self, stack, nc, name):
        self.sem = stack.enter_context(nc.semaphore(name))
        self.n = 0

    def bump(self, k=1):
        self.n += k
        return self.n


class Plan:
    def __init__(self):
        self.items = []
        self.seq = 0

    def em(self, t, eng, fn):
        self.items.append((t, self.seq, eng, fn))
        self.seq += 1

    def run(self, eng_name, eng):
        for t, s, e, fn in sorted(self.items):
            if e == eng_name:
                fn(eng)


def build_nc():
    nc = bass.Bass("TRN2", target_bir_lowering=False)

    d_xt = nc.dram_tensor("x_t", [TSH, D], F32, kind="ExternalInput")
    d_xTd = nc.dram_tensor("xT_d", [DSH, T], F32, kind="ExternalInput")
    d_wg = nc.dram_tensor("wg_t", [MB, 2, 128, KH * 128], F16, kind="ExternalInput")
    d_wu = nc.dram_tensor("wu_t", [MB, 2, 128, KH * 128], F16, kind="ExternalInput")
    d_wd = nc.dram_tensor("wd_t", [NJ, 128, NBAT * NB * 128], F16, kind="ExternalInput")
    d_xpw = nc.dram_tensor("xpwT", [D, 160], F32, kind="ExternalInput")
    d_dtw = nc.dram_tensor("dtwT", [RK, DSH], F32, kind="ExternalInput")
    d_dtb = nc.dram_tensor("dtb", [128, NB], F32, kind="ExternalInput")
    d_Dd = nc.dram_tensor("Dd", [128, NB], F32, kind="ExternalInput")
    d_Al = nc.dram_tensor("Alog", [128, NB * NS], F32, kind="ExternalInput")
    d_out = nc.dram_tensor("out_dT", [DSH, T], F32, kind="ExternalOutput")

    xn16d = nc.dram_tensor("xn16d", [TSH, D], F16)
    s_dram = nc.dram_tensor("s_dram", [DSH, T], F32)
    cst_i = nc.dram_tensor("cst_i", [1, 2 * TSH], F32)
    cst_o = nc.dram_tensor("cst_o", [R, 2 * TSH], F32)
    cxp_i = nc.dram_tensor("cxp_i", [160, TSH], F32)
    cxp_o = nc.dram_tensor("cxp_o", [R * 160, TSH], F32)
    cht_i = [nc.dram_tensor(f"cht_i{q}", [M, TQ], F16) for q in range(QN)]
    cht_o = [nc.dram_tensor(f"cht_o{q}", [R * M, TQ], F16) for q in range(QN)]
    if DEBUG:
        dbg_xn = nc.dram_tensor("dbg_xn", [TSH, D], F16, kind="ExternalOutput")
        dbg_s = nc.dram_tensor("dbg_s", [DSH, T], F32, kind="ExternalOutput")
        dbg_ht = nc.dram_tensor("dbg_ht", [M, TQ], F16, kind="ExternalOutput")
        dbg_xp = nc.dram_tensor("dbg_xp", [R * 160, TSH], F32,
                                kind="ExternalOutput")

    rg = [list(range(R))]

    with ExitStack() as outer:
        sb = lambda name, shape, dt: outer.enter_context(
            nc.sbuf_tensor(name, shape, dt))
        xnT16 = sb("xnT16", [128, KB * TQ], F16)
        A_sb = sb("A_sb", [128, NB * NS], F32)
        D_sb = sb("D_sb", [128, NB], F32)
        dtb_sb = sb("dtb_sb", [128, NB], F32)
        dtw_sb = sb("dtw_sb", [128, DSH], F32)
        carry = sb("carry", [128, NB * NS], F32)
        cst_sb = sb("cst_sb", [128, 2], F32)

        ch = lambda name: Ch(outer, nc, name)
        c_st = ch("c_st")
        c_xp = ch("c_xp")

        with ExitStack() as b1:
            sb1 = lambda name, shape, dt: b1.enter_context(
                nc.sbuf_tensor(name, shape, dt))
            xti = [sb1(f"xti{j}", [128, D], F32) for j in range(2)]
            xsq = sb1("xsq", [128, D], F32)
            xn16t = sb1("xn16t", [128, D], F16)
            colA = sb1("colA", [128, 8], F32)
            stats_sb = sb1("stats_sb", [128, 16], F32)
            xpw16 = sb1("xpw16", [128, KB * 160], F16)
            xpwst = sb1("xpwst", [128, 160], F32)
            stg1 = sb1("stg1", [128, TQ], F32)
            stg2 = sb1("stg2", [32, TQ], F32)
            p_x1 = b1.enter_context(nc.psum_tensor("p_x1", [128, TQ], F32))
            p_x2 = b1.enter_context(nc.psum_tensor("p_x2", [32, TQ], F32))

            l_xt = [ch("l_xt0"), ch("l_xt1")]; l_xn = ch("l_xn"); l_tr = ch("l_tr")
            l_st = ch("l_st"); l_xpw = ch("l_xpw"); l_xps = ch("l_xps")
            v1 = ch("v1"); s1 = ch("s1"); pe1 = ch("pe1"); g1 = ch("g1")
            vsm = ch("vsm"); ssm = ch("ssm")

            NT = TSH // 128
            with nc.Block() as blk:
                @blk.sync
                def _(s):
                    for i in range(NT):
                        if i >= 2:
                            s.wait_ge(v1.sem, 3 * (i - 1))
                        s.dma_start(out=xti[i % 2][:, :],
                                    in_=d_xt.ap()[i * 128:(i + 1) * 128, :]
                                    ).then_inc(l_xt[i % 2].sem, 16)
                        l_xt[i % 2].bump(16)
                        s.wait_ge(v1.sem, 3 * i + 3)
                        s.dma_start(out=xn16d.ap()[i * 128:(i + 1) * 128, :],
                                    in_=xn16t[:, :]).then_inc(l_xn.sem, 16)
                        l_xn.bump(16)
                    s.wait_ge(l_xn.sem, 16 * NT)
                    for q in range(QN):
                        if q >= 1:
                            s.wait_ge(pe1.sem, 2 * q)
                        for db in range(KB):
                            s.dma_start(
                                out=xnT16[:, db * TQ:(db + 1) * TQ],
                                in_=xn16d.ap()[q * TQ:(q + 1) * TQ,
                                               db * 128:(db + 1) * 128],
                                transpose=True).then_inc(l_tr.sem, 16)
                            l_tr.bump(16)

                @blk.vector
                def _(v):
                    for i in range(NT):
                        xt = xti[i % 2]
                        v.wait_ge(l_xt[i % 2].sem, 16 * (i // 2 + 1))
                        def step(ins_):
                            ins_.then_inc(vsm.sem, 1)
                            v.wait_ge(vsm.sem, vsm.bump())
                        step(v.reduce_sum(colA[:, 0:1], xt[:, :],
                                          axis=mybir.AxisListType.X))
                        v.wait_ge(s1.sem, 2 * i + 1)
                        step(v.reduce_sum(colA[:, 1:2], xsq[:, :],
                                          axis=mybir.AxisListType.X))
                        step(v.tensor_scalar_mul(stats_sb[:, 2 * i:2 * i + 1],
                                                 colA[:, 0:1], -1.0 / D))
                        step(v.tensor_scalar(colA[:, 3:4],
                                             stats_sb[:, 2 * i:2 * i + 1],
                                             stats_sb[:, 2 * i:2 * i + 1],
                                             None, OP.mult))
                        step(v.tensor_scalar_mul(colA[:, 4:5], colA[:, 1:2],
                                                 1.0 / D))
                        step(v.scalar_tensor_tensor(colA[:, 5:6], colA[:, 3:4],
                                                    -1.0, colA[:, 4:5],
                                                    OP.mult, OP.add))
                        v.tensor_scalar_add(colA[:, 5:6], colA[:, 5:6], 1e-5
                                            ).then_inc(v1.sem, 1)
                        v1.bump()
                        v.wait_ge(s1.sem, 2 * i + 2)
                        v.engine_nop().then_inc(v1.sem, 1)
                        v1.bump()
                        if i >= 1:
                            v.wait_ge(l_xn.sem, 16 * i)
                        v.tensor_scalar(xn16t[:, :], xt[:, :],
                                        stats_sb[:, 2 * i:2 * i + 1],
                                        stats_sb[:, 2 * i + 1:2 * i + 2],
                                        OP.add, OP.mult).then_inc(v1.sem, 1)
                        v1.bump()
                    v.wait_ge(s1.sem, 2 * NT + KB + 1)
                    v.tensor_scalar_mul(A_sb[:, :], A_sb[:, :], -1.0
                                        ).then_inc(vsm.sem, 1)
                    v.wait_ge(vsm.sem, vsm.bump())
                    v.memset(carry[:, :], 0.0).then_inc(v1.sem, 1)
                    v1.bump()

                @blk.scalar
                def _(sc):
                    for i in range(NT):
                        sc.wait_ge(l_xt[i % 2].sem, 16 * (i // 2 + 1))
                        if i >= 1:
                            sc.wait_ge(v1.sem, 3 * (i - 1) + 1)
                        sc.activation(xsq[:, :], xti[i % 2][:, :], AF.Square
                                      ).then_inc(s1.sem, 1)
                        s1.bump()
                        sc.wait_ge(v1.sem, 3 * i + 1)
                        if i == 0:
                            sc.wait_ge(g1.sem, 1)
                        sc.activation(colA[:, 6:7], colA[:, 5:6], AF.Ln
                                      ).then_inc(ssm.sem, 1)
                        sc.wait_ge(ssm.sem, ssm.bump())
                        sc.activation(stats_sb[:, 2 * i + 1:2 * i + 2],
                                      colA[:, 6:7], AF.Exp,
                                      scale=cst_sb[:, 1:2]).then_inc(s1.sem, 1)
                        s1.bump()
                    sc.wait_ge(v1.sem, 3 * (NT - 1) + 1)
                    for i in range(NT):
                        sc.dma_start(out=cst_i.ap()[0:1, i * 128:(i + 1) * 128],
                                     in_=stats_sb[:, 2 * i:2 * i + 1]
                                     ).then_inc(l_st.sem, 16)
                        l_st.bump(16)
                        sc.dma_start(
                            out=cst_i.ap()[0:1, TSH + i * 128:TSH + (i + 1) * 128],
                            in_=stats_sb[:, 2 * i + 1:2 * i + 2]
                            ).then_inc(l_st.sem, 16)
                        l_st.bump(16)
                    for kb in range(KB):
                        sc.dma_start(out=xpwst[:, :],
                                     in_=d_xpw.ap()[kb * 128:(kb + 1) * 128, :]
                                     ).then_inc(l_xpw.sem, 16)
                        l_xpw.bump(16)
                        sc.wait_ge(l_xpw.sem, l_xpw.n)
                        sc.copy(xpw16[:, kb * 160:(kb + 1) * 160], xpwst[:, :]
                                ).then_inc(s1.sem, 1)
                        s1.bump()
                    sc.dma_start(out=A_sb[:, :], in_=d_Al.ap()[:, :]
                                 ).then_inc(l_xpw.sem, 16)
                    l_xpw.bump(16)
                    sc.dma_start(out=D_sb[:, :], in_=d_Dd.ap()[:, :]
                                 ).then_inc(l_xpw.sem, 16)
                    l_xpw.bump(16)
                    sc.dma_start(out=dtb_sb[:, :], in_=d_dtb.ap()[:, :]
                                 ).then_inc(l_xpw.sem, 16)
                    l_xpw.bump(16)
                    sc.dma_start(out=dtw_sb[:, :], in_=d_dtw.ap()[:, :]
                                 ).then_inc(l_xpw.sem, 16)
                    l_xpw.bump(16)
                    sc.wait_ge(l_xpw.sem, l_xpw.n)
                    sc.activation(A_sb[:, :], A_sb[:, :], AF.Exp
                                  ).then_inc(s1.sem, 1)
                    s1.bump()
                    for q in range(QN):
                        sc.wait_ge(pe1.sem, 2 * q + 2)
                        sc.copy(stg1[:, :], p_x1[:, :])
                        sc.copy(stg2[:, :], p_x2[:, :]).then_inc(s1.sem, 1)
                        s1.bump()
                        sc.dma_start(out=cxp_i.ap()[0:128, q * TQ:(q + 1) * TQ],
                                     in_=stg1[:, :]).then_inc(l_xps.sem, 16)
                        l_xps.bump(16)
                        sc.dma_start(out=cxp_i.ap()[128:160, q * TQ:(q + 1) * TQ],
                                     in_=stg2[:, :]).then_inc(l_xps.sem, 16)
                        l_xps.bump(16)

                n_s1c = 2 * NT + KB + 1

                @blk.tensor
                def _(t):
                    t.wait_ge(s1.sem, n_s1c)
                    for q in range(QN):
                        t.wait_ge(l_tr.sem, 16 * 32 * (q + 1))
                        t.wait_ge(s1.sem, n_s1c + q)
                        for kb in range(KB):
                            lt = xpw16[:, kb * 160:kb * 160 + 128]
                            lt2 = xpw16[:, kb * 160 + 128:(kb + 1) * 160]
                            rh = xnT16[:, kb * TQ:(kb + 1) * TQ]
                            t.matmul(p_x1[:, :], lt, rh, start=(kb == 0),
                                     stop=(kb == KB - 1))
                            mm = t.matmul(p_x2[:, :], lt2, rh, start=(kb == 0),
                                          stop=(kb == KB - 1))
                        mm.then_inc(pe1.sem, 2)
                        pe1.bump(2)

                @blk.gpsimd
                def _(g):
                    g.memset(cst_sb[:, 0:1], 0.5)
                    g.memset(cst_sb[:, 1:2], -0.5).then_inc(g1.sem, 1)
                    g1.bump()
                    g.wait_ge(l_st.sem, 16 * 16)
                    g.collective_compute("AllGather", OP.bypass, rg,
                                         [cst_i.ap().opt()], [cst_o.ap().opt()]
                                         ).then_inc(c_st.sem, 1)
                    c_st.bump()
                    g.wait_ge(l_xps.sem, 16 * 4)
                    g.collective_compute("AllGather", OP.bypass, rg,
                                         [cxp_i.ap().opt()], [cxp_o.ap().opt()]
                                         ).then_inc(c_xp.sem, 1)
                    c_xp.bump()

        with ExitStack() as b2:
            sb2 = lambda name, shape, dt: b2.enter_context(
                nc.sbuf_tensor(name, shape, dt))
            wg_sb = [sb2(f"wg{j}", [128, KH * 128], F16) for j in range(2)]
            wu_sb = [sb2(f"wu{j}", [128, KH * 128], F16) for j in range(2)]
            ht_al = [xnT16[:, s * 2048:(s + 1) * 2048] for s in range(4)]
            wd_al = [xnT16[:, 8192 + s * 2048:8192 + (s + 1) * 2048]
                     for s in range(4)]
            thg = [sb2(f"thg{j}", [128, TQ], F32) for j in range(2)]
            t1g = sb2("t1g", [128, TQ], F32)
            h16 = [sb2(f"h16{j}", [128, TQ], F16) for j in range(2)]
            thd = [sb2(f"thd{j}", [128, TQ], F32) for j in range(2)]
            t1d = sb2("t1d", [128, TQ], F32)
            s_tl = [sb2(f"s_tl{j}", [128, TQ], F32) for j in range(2)]
            xT_tl = [sb2(f"xT_tl{j}", [128, TQ], F32) for j in range(2)]
            o_sb = [sb2(f"o_sb{j}", [128, TQ], F32) for j in range(2)]
            xn_sb = sb2("xn_sb", [128, NB * TM], F32)
            dt_sb = [sb2(f"dt{j}", [128, NB * TM], F32) for j in range(3)]
            dtx_sb = sb2("dtx", [128, NB * TM], F32)
            sp_o = [sb2(f"sp_o{j}", [128, TQ], F32) for j in range(2)]
            sp_tmp = sb2("sp_tmp", [128, TQ], F32)
            dA_r = [sb2(f"dA{j}", [128, TM], F32) for j in range(2)]
            bh_r = [sb2(f"bh{j}", [128, TM], F32) for j in range(2)]
            B_r = [sb2(f"B{j}", [128, TM], F32) for j in range(2)]
            C_r = [sb2(f"C{j}", [128, TM], F32) for j in range(2)]
            mr_rep = sb2("mr_rep", [128, TM], F32)
            dtin = sb2("dtin", [128, TM], F32)
            hc_tmp = sb2("hc_tmp", [128, TM], F32)

            psum = [b2.enter_context(nc.psum_tensor(f"pb{i}", [128, TQ], F32))
                    for i in range(8)]
            pg, pu, pd = psum[0:2], psum[2:4], psum[4:8]

            d_wgc = [ch("d_wgc0"), ch("d_wgc1")]
            d_wuc = [ch("d_wuc0"), ch("d_wuc1")]
            d_wdc = [ch(f"d_wdc{i}") for i in range(4)]
            d_htr = [ch(f"d_htr{i}") for i in range(4)]
            d_hts = [ch("d_hts0"), ch("d_hts1")]
            d_tr2 = ch("d_tr2")
            d_mld = ch("d_mld")
            d_repB = [ch("d_repB0"), ch("d_repB1")]
            d_repC = [ch("d_repC0"), ch("d_repC1")]
            d_dti = ch("d_dti")
            d_ssw = [ch(f"d_ssw{b}") for b in range(NB)]
            d_stl = [ch("d_stl0"), ch("d_stl1")]
            d_xtt = [ch("d_xtt0"), ch("d_xtt1")]
            d_outc = [ch("d_outc0"), ch("d_outc1")]
            c_h = [ch("c_h0"), ch("c_h1")]
            pe_g = ch("pe_g"); pe_u = ch("pe_u"); pe_dn = ch("pe_dn")
            pe_dnb = ch("pe_dnb"); pe_dt = ch("pe_dt")
            ev_g = ch("ev_g"); ev_u = ch("ev_u"); ev_dn = ch("ev_dn")
            ev_dtb = ch("ev_dtb")
            sc_th = ch("sc_th"); sc_thd = ch("sc_thd"); sc_sp = ch("sc_sp")
            sc_sp2 = ch("sc_sp2")
            sc_dA = ch("sc_dA"); sc_car = ch("sc_car")
            v_scan = ch("v_scan"); v_yin = ch("v_yin"); v_sp = ch("v_sp")
            v_dnev = ch("v_dnev"); v_prep = ch("v_prep")
            gp_hc = ch("gp_hc"); gp_acc = ch("gp_acc")

            P = Plan()

            for q in range(QN if 'gateup' not in SKIP else 0):
                def refill(s, q=q):
                    if q >= 1:
                        s.wait_ge(pe_g.sem, 2 * MB)
                        s.wait_ge(pe_u.sem, 2 * MB)
                    for db in range(KB):
                        s.dma_start(
                            out=xnT16[:, db * TQ:(db + 1) * TQ],
                            in_=xn16d.ap()[q * TQ:(q + 1) * TQ,
                                           db * 128:(db + 1) * 128],
                            transpose=True).then_inc(d_tr2.sem, 16)
                P.em(q * MB * GI_US - 32.0, 'sync', refill)
                d_tr2.bump(16 * KB)

                for mb in range(MB):
                    gi = q * MB + mb
                    tt = gi * GI_US
                    for h in range(2):
                        j = 2 * gi + h
                        def wload(s, q=q, mb=mb, h=h, j=j):
                            if j >= 2:
                                s.wait_ge(pe_g.sem, j - 1)
                            s.dma_start(
                                out=wg_sb[j % 2][:, :],
                                in_=d_wg.ap()[mb, h, :, :]
                                ).then_inc(d_wgc[j % 2].sem, 16)
                            if j >= 2:
                                s.wait_ge(pe_u.sem, j - 1)
                            s.dma_start(
                                out=wu_sb[j % 2][:, :],
                                in_=d_wu.ap()[mb, h, :, :]
                                ).then_inc(d_wuc[j % 2].sem, 16)
                        P.em(tt - 30.0 + h * 3.0, 'sync', wload)
                        d_wgc[j % 2].bump(16); d_wuc[j % 2].bump(16)

                    def gup(t, q=q, mb=mb, gi=gi):
                        t.wait_ge(d_tr2.sem, 16 * KB * (q + 1))
                        for h in range(2):
                            j = 2 * gi + h
                            t.wait_ge(d_wgc[j % 2].sem, 16 * (j // 2 + 1))
                            if gi >= 2 and h == 0:
                                t.wait_ge(ev_g.sem, gi - 1)
                            for k in range(KH):
                                kb = h * KH + k
                                mm = t.matmul(
                                    pg[gi % 2][:, :],
                                    wg_sb[j % 2][:, k * 128:(k + 1) * 128],
                                    xnT16[:, kb * TQ:(kb + 1) * TQ],
                                    start=(kb == 0), stop=(kb == KB - 1))
                            mm.then_inc(pe_g.sem, 1)
                        for h in range(2):
                            j = 2 * gi + h
                            t.wait_ge(d_wuc[j % 2].sem, 16 * (j // 2 + 1))
                            if gi >= 2 and h == 0:
                                t.wait_ge(ev_u.sem, gi - 1)
                            for k in range(KH):
                                kb = h * KH + k
                                mm = t.matmul(
                                    pu[gi % 2][:, :],
                                    wu_sb[j % 2][:, k * 128:(k + 1) * 128],
                                    xnT16[:, kb * TQ:(kb + 1) * TQ],
                                    start=(kb == 0), stop=(kb == KB - 1))
                            mm.then_inc(pe_u.sem, 1)
                    P.em(tt, 'tensor', gup)
                    pe_g.bump(2); pe_u.bump(2)

                    def tanh_g(sc, gi=gi):
                        sc.wait_ge(pe_g.sem, 2 * gi + 2)
                        if gi >= 2:
                            sc.wait_ge(ev_g.sem, gi - 1)
                        sc.activation(thg[gi % 2][:, :], pg[gi % 2][:, :],
                                      AF.Tanh, scale=cst_sb[:, 0:1]
                                      ).then_inc(sc_th.sem, 1)
                    P.em(tt + 6.0, 'scalar', tanh_g)
                    sc_th.bump()

                    def t1_g(v, gi=gi):
                        v.wait_ge(sc_th.sem, gi + 1)
                        v.scalar_tensor_tensor(t1g[:, :], thg[gi % 2][:, :],
                                               1.0, pg[gi % 2][:, :], OP.add,
                                               OP.mult).then_inc(ev_g.sem, 1)
                        v.wait_ge(pe_u.sem, 2 * gi + 2)
                        if gi >= 2:
                            v.wait_ge(d_hts[gi % 2].sem, 16 * (gi // 2))
                        v.scalar_tensor_tensor(h16[gi % 2][:, :], t1g[:, :],
                                               0.5, pu[gi % 2][:, :], OP.mult,
                                               OP.mult).then_inc(ev_u.sem, 1)
                    P.em(tt + 8.0, 'vector', t1_g)
                    ev_g.bump(); ev_u.bump()

                    def hts(sc, q=q, mb=mb, gi=gi):
                        sc.wait_ge(ev_u.sem, gi + 1)
                        sc.dma_start(
                            out=cht_i[q].ap()[mb * 128:(mb + 1) * 128, :],
                            in_=h16[gi % 2][:, :]
                            ).then_inc(d_hts[gi % 2].sem, 16)
                    P.em(tt + 10.0, 'scalar', hts)
                    d_hts[gi % 2].bump(16)

            for q in range(QN if 'gateup' not in SKIP else 0):
                def agh(g, q=q):
                    g.wait_ge(d_hts[0].sem, 16 * (MB // 2) * (q + 1))
                    g.wait_ge(d_hts[1].sem, 16 * (MB // 2) * (q + 1))
                    g.collective_compute(
                        "AllGather", OP.bypass, rg,
                        [cht_i[q].ap().opt()], [cht_o[q].ap().opt()]
                        ).then_inc(c_h[q].sem, 1)
                P.em((4.0 + 1.6 * q) * MCH_US, 'gpsimd', agh)
                c_h[q].bump()

            scan_i = 0
            for c in range(CN if 'mamba' not in SKIP else 0):
                tc = c * MCH_US
                def dtin_ld(sc, c=c):
                    sc.wait_ge(c_xp.sem, 1)
                    if c >= 1:
                        sc.wait_ge(pe_dt.sem, 2 * NB * c)
                    sc.dma_start(out=dtin[:, :],
                                 in_=cxp_o.ap()[160 * c:160 * c + 128, :]
                                 ).then_inc(d_dti.sem, 16)
                P.em(tc - 2 * MCH_US + 300.0, 'sync', dtin_ld)
                d_dti.bump(16)

                for b in range(NB):
                    def dt_mm(t, c=c, b=b):
                        t.wait_ge(d_dti.sem, 16 * (c + 1))
                        bank = pd if c < CN - 1 else pg
                        if c == CN - 1 and b == 0 and 'gateup' not in SKIP:
                            t.wait_ge(ev_g.sem, QN * MB)
                        for h in range(2):
                            j = (c * NB + b) * 2 + h
                            t.wait_ge(ev_dtb.sem, max(0, j - 1))
                            t.matmul(bank[j % 2][:, :],
                                     dtw_sb[:, b * 128:(b + 1) * 128],
                                     dtin[:, h * TQ:(h + 1) * TQ],
                                     start=True, stop=True
                                     ).then_inc(pe_dt.sem, 1)
                    P.em((DN0 - 45.0 + b) if c == CN - 1 else
                         (tc - MCH_US + 17.0 + b), 'tensor', dt_mm)
                    pe_dt.bump(2)

                for b in range(NB):
                    def sp_sc(sc, c=c, b=b):
                        ib = c * NB + b
                        sc.wait_ge(pe_dt.sem, 2 * ib + 2)
                        if ib >= 1:
                            sc.wait_ge(v_sp.sem, 2 * ib)
                        if 2 <= c < CN - 1:
                            sc.wait_ge(sc_dA.sem, (c - 1) * NS * NB)
                        bank = pd if c < CN - 1 else pg
                        for h in range(2):
                            j = ib * 2 + h
                            cs = b * TM + h * TQ
                            sc.activation(dt_sb[2 if c == CN - 1 else c % 2][:, cs:cs + TQ],
                                          bank[j % 2][:, :], AF.Relu,
                                          bias=dtb_sb[:, b:b + 1])
                            sc.activation(sp_o[h][:, :], bank[j % 2][:, :],
                                          AF.Abs, bias=dtb_sb[:, b:b + 1]
                                          ).then_inc(ev_dtb.sem, 1)
                            sc.wait_ge(ev_dtb.sem, ib * 2 + h + 1)
                            sc.activation(sp_tmp[:, :], sp_o[h][:, :], AF.Exp,
                                          scale=-1.0).then_inc(sc_sp2.sem, 1)
                            sc.wait_ge(sc_sp2.sem, sc_sp2.bump())
                            sc.activation(sp_o[h][:, :], sp_tmp[:, :], AF.Ln,
                                          bias=1.0).then_inc(sc_sp.sem, 1)
                    P.em((DN0 - 44.5 + b) if c == CN - 1 else
                         (tc - MCH_US + 20.0 + b), 'scalar', sp_sc)
                    ev_dtb.bump(2)
                    for h in range(2):
                        cnt = sc_sp.bump()
                        def sp_v(v, c=c, b=b, h=h, cnt=cnt):
                            v.wait_ge(sc_sp.sem, cnt)
                            cs2 = b * TM + h * TQ
                            v.tensor_add(dt_sb[2 if c == CN - 1 else c % 2][:, cs2:cs2 + TQ],
                                         dt_sb[2 if c == CN - 1 else c % 2][:, cs2:cs2 + TQ],
                                         sp_o[h][:, :]).then_inc(v_sp.sem, 1)
                        P.em((DN0 - 43.9 + b + 0.1 * h) if c == CN - 1
                             else (tc - MCH_US + 20.6 + b + 0.1 * h), 'vector',
                             sp_v)
                    v_sp.bump(2)

                for b in range(NB):
                    def mldb(sc, c=c, b=b):
                        if c == 0:
                            sc.wait_ge(c_st.sem, 1)
                        else:
                            sc.wait_ge(d_ssw[b].sem, 16 * c)
                        sc.dma_start(out=xn_sb[:, b * TM:(b + 1) * TM],
                                     in_=d_xTd.ap()[b * 128:(b + 1) * 128,
                                                    c * TM:(c + 1) * TM]
                                     ).then_inc(d_mld.sem, 16)
                    P.em((tc - 540.0 + b) if c == 0 else (tc - 9.5 + b),
                         'scalar', mldb)
                    d_mld.bump(16)

                def mld_m(sc, c=c):
                    if c >= 1:
                        sc.wait_ge(v_prep.sem, 3 * c - 1)
                    sc.dma_start(out=mr_rep[:, :],
                                 in_=cst_o.ap()[c:c + 1, 0:TM
                                                ].broadcast_to((128, TM))
                                 ).then_inc(d_mld.sem, 16)
                P.em(tc - 5.0 if c > 0 else tc - 535.0, 'scalar', mld_m)
                d_mld.bump(16)

                def rld(sc, c=c):
                    sc.wait_ge(v_prep.sem, 3 * c + 1)
                    sc.dma_start(out=mr_rep[:, :],
                                 in_=cst_o.ap()[c:c + 1, TSH:TSH + TM
                                                ].broadcast_to((128, TM))
                                 ).then_inc(d_mld.sem, 16)
                P.em(tc + 0.2, 'scalar', rld)
                d_mld.bump(16)

                def prep1(v, c=c):
                    v.wait_ge(d_mld.sem, 16 * (c * (NB + 2) + NB + 1))
                    if c == 0:
                        v.wait_ge(v1.sem, v1.n)
                    for b in range(NB):
                        bs = b * TM
                        v.tensor_add(xn_sb[:, bs:bs + TM], xn_sb[:, bs:bs + TM],
                                     mr_rep[:, :])
                    v.engine_nop().then_inc(v_prep.sem, 1)
                P.em(tc + 0.1, 'vector', prep1)
                v_prep.bump()

                def prep2(v, c=c):
                    v.wait_ge(d_mld.sem, 16 * (c + 1) * (NB + 2))
                    for b in range(NB):
                        bs = b * TM
                        v.tensor_mul(xn_sb[:, bs:bs + TM], xn_sb[:, bs:bs + TM],
                                     mr_rep[:, :])
                    v.engine_nop().then_inc(v_prep.sem, 1)
                    for b in range(NB):
                        bs = b * TM
                        v.wait_ge(v_sp.sem, (c * NB + b + 1) * 2)
                        v.tensor_mul(dtx_sb[:, bs:bs + TM],
                                     dt_sb[2 if c == CN - 1 else c % 2][:, bs:bs + TM],
                                     xn_sb[:, bs:bs + TM])
                        v.tensor_scalar(xn_sb[:, bs:bs + TM],
                                        xn_sb[:, bs:bs + TM], D_sb[:, b:b + 1],
                                        None, OP.mult).then_inc(v_yin.sem, 1)
                    v.engine_nop().then_inc(v_prep.sem, 1)
                P.em(tc + 0.4, 'vector', prep2)
                v_prep.bump(2); v_yin.bump(NB)

                for n in range(NS):
                    ri = c * NS + n
                    def repl(sc, c=c, n=n, ri=ri):
                        if ri >= 2:
                            sc.wait_ge(v_scan.sem, (ri - 2) * NB + NB)
                            sc.wait_ge(gp_hc.sem, (ri - 2) * NB + NB)
                        sc.dma_start(out=B_r[n % 2][:, :],
                                     in_=cxp_o.ap()[160 * c + 128 + n:
                                                    160 * c + 129 + n, :
                                                    ].broadcast_to((128, TM))
                                     ).then_inc(d_repB[n % 2].sem, 16)
                        sc.dma_start(out=C_r[n % 2][:, :],
                                     in_=cxp_o.ap()[160 * c + 144 + n:
                                                    160 * c + 145 + n, :
                                                    ].broadcast_to((128, TM))
                                     ).then_inc(d_repC[n % 2].sem, 16)
                    P.em(tc + 0.2 + n * 30.0, 'scalar', repl)
                    d_repB[n % 2].bump(16); d_repC[n % 2].bump(16)

                    for b in range(NB):
                        i = scan_i
                        col = b * NS + n
                        def dA_f(sc, c=c, b=b, i=i, col=col):
                            if i >= 2:
                                sc.wait_ge(v_scan.sem, i - 1)
                            sc.wait_ge(v_sp.sem, (c * NB + b + 1) * 2)
                            sc.activation(dA_r[i % 2][:, :],
                                          dt_sb[2 if c == CN - 1 else c % 2][:, b * TM:(b + 1) * TM],
                                          AF.Exp, scale=A_sb[:, col:col + 1]
                                          ).then_inc(sc_dA.sem, 1)
                        P.em(tc + 1.0 + (n * NB + b) * 8.0, 'scalar', dA_f)
                        sc_dA.bump()

                        def dbx_scan(v, c=c, b=b, n=n, i=i, col=col, ri=ri):
                            v.wait_ge(d_repB[n % 2].sem, 16 * (ri // 2 + 1))
                            if i >= 2:
                                v.wait_ge(gp_hc.sem, i - 1)
                            v.tensor_mul(bh_r[i % 2][:, :],
                                         dtx_sb[:, b * TM:(b + 1) * TM],
                                         B_r[n % 2][:, :])
                            v.wait_ge(sc_dA.sem, i + 1)
                            if i >= NS * NB:
                                v.wait_ge(sc_car.sem, i - NS * NB + 1)
                            v.tensor_tensor_scan(bh_r[i % 2][:, :],
                                                 dA_r[i % 2][:, :],
                                                 bh_r[i % 2][:, :],
                                                 carry[:, col:col + 1],
                                                 OP.mult, OP.add
                                                 ).then_inc(v_scan.sem, 1)
                        P.em(tc + 2.0 + (n * NB + b) * 8.0, 'vector', dbx_scan)
                        v_scan.bump()

                        def car_f(sc, i=i, col=col):
                            sc.wait_ge(v_scan.sem, i + 1)
                            sc.copy(carry[:, col:col + 1],
                                    bh_r[i % 2][:, TM - 1:TM]
                                    ).then_inc(sc_car.sem, 1)
                        P.em(tc + 4.0 + (n * NB + b) * 8.0, 'scalar', car_f)
                        sc_car.bump()

                        def hc_f(g, c=c, b=b, n=n, i=i, ri=ri):
                            g.wait_ge(v_scan.sem, i + 1)
                            g.wait_ge(d_repC[n % 2].sem, 16 * (ri // 2 + 1))
                            g.tensor_mul(hc_tmp[:, :], bh_r[i % 2][:, :],
                                         C_r[n % 2][:, :]).then_inc(gp_hc.sem, 1)
                            if n == 0:
                                g.wait_ge(v_yin.sem, c * NB + b + 1)
                            g.tensor_add(xn_sb[:, b * TM:(b + 1) * TM],
                                         xn_sb[:, b * TM:(b + 1) * TM],
                                         hc_tmp[:, :]).then_inc(gp_acc.sem, 1)
                        P.em(tc + 5.0 + (n * NB + b) * 8.0, 'gpsimd', hc_f)
                        gp_hc.bump(); gp_acc.bump()
                        scan_i += 1

                for b in range(NB):
                    def sstore(sc, c=c, b=b):
                        sc.wait_ge(gp_acc.sem,
                                   c * NS * NB + (NS - 1) * NB + b + 1)
                        sc.dma_start(out=s_dram.ap()[b * 128:(b + 1) * 128,
                                                     c * TM:(c + 1) * TM],
                                     in_=xn_sb[:, b * TM:(b + 1) * TM]
                                     ).then_inc(d_ssw[b].sem, 16)
                    P.em(tc + MCH_US - 10.0 + b, 'scalar', sstore)
                    d_ssw[b].bump(16)

            nr_dn = 0 if 'down' in SKIP else (1 if 'down1' in SKIP else R)
            nq_dn = 1 if 'down1' in SKIP else QN
            for r in range(nr_dn):
                for q in range(nq_dn):
                    ti = 2 * r + q
                    tt = DN0 + ti * 96.0
                    for J in range(NJ):
                        bi = ti * NJ + J
                        def htload(s, q=q, r=r, J=J, bi=bi):
                            s.wait_ge(c_h[q].sem, 1)
                            if bi < 4:
                                s.wait_ge(pe_g.sem, 2 * QN * MB)
                                s.wait_ge(pe_u.sem, 2 * QN * MB)
                            else:
                                s.wait_ge(pe_dnb.sem, bi - 3)
                            base = r * M + J * NBAT * 128
                            s.dma_start(
                                out=ht_al[bi % 4][:, :],
                                in_=cht_o[q].ap()[base:base + NBAT * 128, :
                                                  ].rearrange(
                                    "(a p) t -> p a t", p=128)
                                ).then_inc(d_htr[bi % 4].sem, 16)
                        P.em(tt + J * 3.4 - 40.0, 'sync', htload)
                        d_htr[bi % 4].bump(16)

                        def wdload(sc, J=J, bi=bi):
                            if bi < 4:
                                sc.wait_ge(pe_g.sem, 2 * QN * MB)
                                sc.wait_ge(pe_u.sem, 2 * QN * MB)
                            else:
                                sc.wait_ge(pe_dnb.sem, bi - 3)
                            sc.dma_start(
                                out=wd_al[bi % 4][:, :],
                                in_=d_wd.ap()[J, :, :]
                                ).then_inc(d_wdc[bi % 4].sem, 16)
                        P.em(tt + J * 2.5 + 5.0, 'scalar', wdload)
                        d_wdc[bi % 4].bump(16)

                        def dmm(t, q=q, r=r, J=J, bi=bi, ti=ti):
                            t.wait_ge(d_htr[bi % 4].sem, 16 * (bi // 4 + 1))
                            t.wait_ge(d_wdc[bi % 4].sem, 16 * (bi // 4 + 1))
                            if J == 0:
                                if ti == 0:
                                    t.wait_ge(pe_dt.sem, 2 * NB * (CN - 1))
                                    t.wait_ge(ev_dtb.sem, 2 * NB * (CN - 1))
                                else:
                                    t.wait_ge(ev_dn.sem, ti * NB)
                            for jj in range(NBAT):
                                for hb in range(NB):
                                    mm = t.matmul(
                                        pd[hb][:, :],
                                        wd_al[bi % 4][:,
                                                      (jj * NB + hb) * 128:
                                                      (jj * NB + hb + 1) * 128],
                                        ht_al[bi % 4][:, jj * TQ:(jj + 1) * TQ],
                                        start=(J == 0 and jj == 0),
                                        stop=(J == NJ - 1 and jj == NBAT - 1))
                            mm.then_inc(pe_dnb.sem, 1)
                        P.em(tt + J * 3.4, 'tensor', dmm)
                        pe_dnb.bump()

                    if 'noevac' in SKIP:
                        def evmin(v, ti=ti):
                            v.wait_ge(pe_dnb.sem, (ti + 1) * NJ)
                            for hb in range(NB):
                                v.tensor_copy(o_sb[0][:, :], pd[hb][:, :]
                                              ).then_inc(ev_dn.sem, 1)
                        P.em(DN0 + ti * 96.0 + 92.0, 'vector', evmin)
                        ev_dn.bump(NB)
                        continue
                    for hb in range(NB):
                        ei = ti * NB + hb
                        st_time = tt + 88.6 + hb * 2.0
                        n_prior = sum(1 for c2 in range(CN)
                                      if (c2 + 1) * MCH_US - 10.0 + hb < st_time)
                        n_prior = max(n_prior, r + 1)

                        def stile(sc, q=q, r=r, hb=hb, ei=ei, n_prior=n_prior):
                            sc.wait_ge(d_ssw[hb].sem, 16 * n_prior)
                            if ei >= 2:
                                sc.wait_ge(v_dnev.sem, ei - 1)
                            sc.dma_start(
                                out=s_tl[ei % 2][:, :],
                                in_=s_dram.ap()[hb * 128:(hb + 1) * 128,
                                                r * TSH + q * TQ:
                                                r * TSH + (q + 1) * TQ]
                                ).then_inc(d_stl[ei % 2].sem, 16)
                            sc.dma_start(
                                out=xT_tl[ei % 2][:, :],
                                in_=d_xTd.ap()[hb * 128:(hb + 1) * 128,
                                               r * TSH + q * TQ:
                                               r * TSH + (q + 1) * TQ]
                                ).then_inc(d_xtt[ei % 2].sem, 16)
                        P.em(st_time, 'scalar', stile)
                        d_stl[ei % 2].bump(16); d_xtt[ei % 2].bump(16)

                        def tanh_d(sc, ti=ti, hb=hb, ei=ei):
                            sc.wait_ge(pe_dnb.sem, (ti + 1) * NJ)
                            if ei >= 2:
                                sc.wait_ge(ev_dn.sem, ei - 1)
                            sc.activation(thd[ei % 2][:, :], pd[hb][:, :],
                                          AF.Tanh, scale=cst_sb[:, 0:1]
                                          ).then_inc(sc_thd.sem, 1)
                        P.em(tt + 90.0 + hb * 2.0, 'scalar', tanh_d)
                        sc_thd.bump()

                        def ev_d(v, ti=ti, hb=hb, ei=ei):
                            v.wait_ge(sc_thd.sem, ei + 1)
                            v.scalar_tensor_tensor(t1d[:, :], thd[ei % 2][:, :],
                                                   1.0, pd[hb][:, :], OP.add,
                                                   OP.mult).then_inc(ev_dn.sem, 1)
                            v.wait_ge(d_stl[ei % 2].sem, 16 * (ei // 2 + 1))
                            v.scalar_tensor_tensor(t1d[:, :], t1d[:, :], 0.5,
                                                   s_tl[ei % 2][:, :],
                                                   OP.mult, OP.mult)
                            v.wait_ge(d_xtt[ei % 2].sem, 16 * (ei // 2 + 1))
                            if ei >= 2:
                                v.wait_ge(d_outc[ei % 2].sem, 16 * (ei // 2))
                            v.tensor_add(o_sb[ei % 2][:, :], t1d[:, :],
                                         xT_tl[ei % 2][:, :]
                                         ).then_inc(v_dnev.sem, 1)
                        P.em(tt + 92.0 + hb * 2.0, 'vector', ev_d)
                        ev_dn.bump(); v_dnev.bump()

                        def ostore(sc, q=q, r=r, hb=hb, ei=ei):
                            sc.wait_ge(v_dnev.sem, ei + 1)
                            sc.dma_start(
                                out=d_out.ap()[hb * 128:(hb + 1) * 128,
                                               r * TSH + q * TQ:
                                               r * TSH + (q + 1) * TQ],
                                in_=o_sb[ei % 2][:, :]
                                ).then_inc(d_outc[ei % 2].sem, 16)
                        P.em(tt + 94.0 + hb * 2.0, 'scalar', ostore)
                        d_outc[ei % 2].bump(16)

            def fin_sc(sc):
                sc.wait_ge(d_outc[0].sem, d_outc[0].n)
                sc.wait_ge(d_outc[1].sem, d_outc[1].n)
            P.em(1e9, 'scalar', fin_sc)

            def fin_g(g):
                g.wait_ge(d_outc[0].sem, d_outc[0].n)
                g.wait_ge(d_outc[1].sem, d_outc[1].n)
                if DEBUG:
                    g.dma_start(out=dbg_xn.ap()[:, :], in_=xn16d.ap()[:, :]
                                ).then_inc(gp_hc.sem, 16)
                    g.dma_start(out=dbg_s.ap()[:, :], in_=s_dram.ap()[:, :]
                                ).then_inc(gp_hc.sem, 16)
                    g.dma_start(out=dbg_ht.ap()[:, :], in_=cht_i[0].ap()[:, :]
                                ).then_inc(gp_hc.sem, 16)
                    g.dma_start(out=dbg_xp.ap()[:, :], in_=cxp_o.ap()[:, :]
                                ).then_inc(gp_hc.sem, 16)
                    g.wait_ge(gp_hc.sem, gp_hc.n + 64)
            P.em(1e9, 'gpsimd', fin_g)

            with nc.Block() as blk2:
                @blk2.sync
                def _(s):
                    P.run('sync', s)

                @blk2.tensor
                def _(t):
                    P.run('tensor', t)

                @blk2.vector
                def _(v):
                    P.run('vector', v)

                @blk2.scalar
                def _(sc):
                    P.run('scalar', sc)

                @blk2.gpsimd
                def _(g):
                    P.run('gpsimd', g)

    return nc


def kernel(L0_attn_x, L0_attn_A_log, L0_attn_D, L0_attn_x_proj_w,
           L0_attn_dt_proj_w, L0_attn_dt_proj_b, L0_ffn_W_gate, L0_ffn_W_up,
           L0_ffn_W_down):
    x = np.ascontiguousarray(np.asarray(L0_attn_x, dtype=np.float32)
                             ).reshape(T, D)
    A_log = np.asarray(L0_attn_A_log, dtype=np.float32)
    Dv = np.asarray(L0_attn_D, dtype=np.float32)
    xpw = np.asarray(L0_attn_x_proj_w, dtype=np.float32)
    dtw = np.asarray(L0_attn_dt_proj_w, dtype=np.float32)
    dtb = np.asarray(L0_attn_dt_proj_b, dtype=np.float32)
    Wg = np.asarray(L0_ffn_W_gate)
    Wu = np.asarray(L0_ffn_W_up)
    Wd = np.asarray(L0_ffn_W_down)

    wg_t = np.ascontiguousarray(
        Wg.reshape(MB, 128, 2, KH, 128).transpose(0, 2, 4, 3, 1)
        ).reshape(MB, 2, 128, KH * 128)
    wu_t = np.ascontiguousarray(
        Wu.reshape(MB, 128, 2, KH, 128).transpose(0, 2, 4, 3, 1)
        ).reshape(MB, 2, 128, KH * 128)
    xpwT = np.ascontiguousarray(xpw.T)
    xT = x.T

    in_maps = []
    for c in range(R):
        ds = slice(c * DSH, (c + 1) * DSH)
        in_maps.append({
            "x_t": np.ascontiguousarray(x[c * TSH:(c + 1) * TSH]),
            "xT_d": np.ascontiguousarray(xT[ds]),
            "wg_t": wg_t,
            "wu_t": wu_t,
            "wd_t": np.ascontiguousarray(
                Wd[ds].reshape(NB, 128, NJ, NBAT, 128).transpose(
                    2, 4, 3, 0, 1)).reshape(NJ, 128, NBAT * NB * 128),
            "xpwT": xpwT,
            "dtwT": np.ascontiguousarray(dtw[ds].T),
            "dtb": np.ascontiguousarray(dtb[ds].reshape(NB, 128).T),
            "Dd": np.ascontiguousarray(Dv[ds].reshape(NB, 128).T),
            "Alog": np.ascontiguousarray(
                A_log[ds].reshape(NB, 128, NS).transpose(1, 0, 2
                                                         ).reshape(128, NB * NS)),
        })

    nc = build_nc()
    res = run_bass_kernel_spmd(nc, in_maps, core_ids=list(range(R)))
    outT = np.concatenate([r["out_dT"] for r in res.results], axis=0)
    out = np.ascontiguousarray(outT.T).reshape(1, T, D).astype(np.float32)
    if DEBUG:
        kernel._dbg = res.results
    return out
